# version 12
# baseline (speedup 1.0000x reference)
from contextlib import ExitStack

import numpy as np
import concourse.bass as bass
import concourse.mybir as mybir
from concourse.bass_utils import run_bass_kernel_spmd

F32 = mybir.dt.float32
BF16 = mybir.dt.bfloat16
AF = mybir.ActivationFunctionType

D = 4096
T = 4096
BATCH = 4
H = 32
DT = D // 128
TOWN = 2048
CH = 512
NCH = TOWN // CH
M2 = 8192
EXTS = [[4, 8, 28, 32], [12, 16, 20, 24]]
MW = 4
Q0S = [[0, 512, 3072, 3584], [1024, 1536, 2048, 2560]]
SCALE = float(np.float32(1.0) / np.float32(np.sqrt(np.float32(128.0))))
ESHIFT = -12.0
EPS = 1e-5

DEBUG_SCRATCH = False
MAX_PHASE = 99


PAR_OVERRIDE = None


def _par_iter(eng, idx):
    if PAR_OVERRIDE is not None:
        return [PAR_OVERRIDE]
    return eng.Switch(idx, 2)


class Sem:
    def __init__(self, nc, es, name):
        self.h = es.enter_context(nc.semaphore(name))
        self.n = 0


def build_program():
    nc = bass.Bass("TRN2", target_bir_lowering=False)
    skind = "ExternalOutput" if DEBUG_SCRATCH else "Internal"

    xT = nc.dram_tensor("xT", [D, T], F32, kind="ExternalInput")
    xTown = nc.dram_tensor("xTown", [D, TOWN], F32, kind="ExternalInput")
    wqT = nc.dram_tensor("wqT", [D, D], F32, kind="ExternalInput")
    wkvT = nc.dram_tensor("wkvT", [D, 256], F32, kind="ExternalInput")
    woT = nc.dram_tensor("woT", [D, D], F32, kind="ExternalInput")
    w12T = nc.dram_tensor("w12T", [D, M2], F32, kind="ExternalInput")
    w3T = nc.dram_tensor("w3T", [D, D], F32, kind="ExternalInput")
    bqd = nc.dram_tensor("bqw", [128, 32], F32, kind="ExternalInput")
    bod = nc.dram_tensor("bow", [128, 32], F32, kind="ExternalInput")
    b12d = nc.dram_tensor("b12w", [128, 64], F32, kind="ExternalInput")
    b3d = nc.dram_tensor("b3w", [128, 32], F32, kind="ExternalInput")
    bkvd = nc.dram_tensor("bkvw", [128, 2], F32, kind="ExternalInput")
    masksd = nc.dram_tensor("masks", [NCH, 128, MW * CH], F32, kind="ExternalInput")
    identd = nc.dram_tensor("identw", [128, 128], F32, kind="ExternalInput")

    qspc = nc.dram_tensor("qspc", [NCH, H, 128, CH], BF16, kind=skind)
    p1sp = nc.dram_tensor("p1sp", [D, TOWN], F32, kind=skind)
    x4sp = nc.dram_tensor("x4sp", [D, TOWN], BF16, kind=skind)
    p2sp = nc.dram_tensor("p2sp", [D, TOWN], F32, kind=skind)
    outT = nc.dram_tensor("outT", [D, TOWN], F32, kind="ExternalOutput")

    with ExitStack() as es:
        ones_bf = es.enter_context(nc.sbuf_tensor("ones_bf", [128, 1], BF16))
        ones_mat = es.enter_context(nc.sbuf_tensor("ones_mat", [128, 128], BF16))
        eshift = es.enter_context(nc.sbuf_tensor("eshift", [128, 1], F32))
        ones1f = es.enter_context(nc.sbuf_tensor("ones1f", [1, 128], F32))
        ident = es.enter_context(nc.sbuf_tensor("ident", [128, 128], BF16))
        bq = es.enter_context(nc.sbuf_tensor("bq", [128, 32], F32))
        bo = es.enter_context(nc.sbuf_tensor("bo", [128, 32], F32))
        b12 = es.enter_context(nc.sbuf_tensor("b12", [128, 64], F32))
        b3 = es.enter_context(nc.sbuf_tensor("b3", [128, 32], F32))
        bkv = es.enter_context(nc.sbuf_tensor("bkv", [128, 2], F32))

        SG = [Sem(nc, es, "s_ga"), Sem(nc, es, "s_gb")]
        SS = [Sem(nc, es, "s_sa"), Sem(nc, es, "s_sb")]
        SO = [Sem(nc, es, "s_oa"), Sem(nc, es, "s_ob")]
        SQ = [Sem(nc, es, "s_qa"), Sem(nc, es, "s_qb")]
        SQR = [Sem(nc, es, "s_qr0"), Sem(nc, es, "s_qr1"), Sem(nc, es, "s_qr2")]
        SMg = Sem(nc, es, "s_mg")
        SMs = Sem(nc, es, "s_ms")
        s_t1 = Sem(nc, es, "s_t1")
        s_t2 = Sem(nc, es, "s_t2")
        s_t3 = Sem(nc, es, "s_t3")
        s_t4 = Sem(nc, es, "s_t4")
        s_t5 = Sem(nc, es, "s_t5")
        s_a1 = Sem(nc, es, "s_a1")
        s_v1 = Sem(nc, es, "s_v1")
        s_v2 = Sem(nc, es, "s_v2")
        s_v3 = Sem(nc, es, "s_v3")

        aux_stacks = []

        def finish():
            for st_ in reversed(aux_stacks):
                st_.close()
            return nc

        Mg0, Ms0 = SMg.n, SMs.n
        with nc.Block("consts") as blk:
            @blk.gpsimd
            def _(g):
                g.dma_start(ident[:], identd[:, :]).then_inc(SMg.h, 16)
                g.memset(ones_bf[:], 1.0)
                g.memset(ones_mat[:], 1.0)
                g.memset(eshift[:], ESHIFT)
                g.memset(ones1f[:], 1.0)

            @blk.scalar
            def _(sc):
                for dst, src in ((bq, bqd), (bo, bod), (b12, b12d), (b3, b3d), (bkv, bkvd)):
                    sc.dma_start(dst[:], src[:, :]).then_inc(SMs.h, 16)
                sc.wait_ge(SMs.h, Ms0 + 80)

            @blk.sync
            def _(sy):
                sy.wait_ge(SMg.h, Mg0 + 16)
        SMg.n += 16
        SMs.n += 80

        es_kv = ExitStack()
        aux_stacks.append(es_kv)
        KT = es_kv.enter_context(nc.sbuf_tensor("KT", [128, T], BF16))
        Vt = es_kv.enter_context(nc.sbuf_tensor("Vt", [128, DT, 128], BF16))

        if MAX_PHASE < 1:
            return finish()
        NKC = 8
        with ExitStack() as ph:
            VTfm = ph.enter_context(nc.sbuf_tensor("VTfm", [128, T], BF16))
            wkv = ph.enter_context(nc.sbuf_tensor("wkv", [128, DT, 256], BF16))
            xtmp = [ph.enter_context(nc.sbuf_tensor(f"xtmp{i}", [128, DT, CH], BF16))
                    for i in range(2)]
            kp = ph.enter_context(nc.psum_tensor("kp", [128, CH], F32))
            vp = ph.enter_context(nc.psum_tensor("vp", [128, CH], F32))
            M0 = SMg.n
            GB = [SG[0].n, SG[1].n]
            T1_0, A1_0 = s_t1.n, s_a1.n

            with nc.Block("p0a") as blk:
                @blk.gpsimd
                def _(g):
                    g.dma_start(wkv[:], wkvT[:, :].rearrange("(a p) o -> p a o", p=128)
                                ).then_inc(SMg.h, 16)
                    for i in range(NKC):
                        if i >= 2:
                            g.wait_ge(s_t1.h, T1_0 + i - 1)
                        g.dma_start(
                            xtmp[i % 2][:],
                            xT[:, i * CH:(i + 1) * CH].rearrange("(a p) t -> p a t", p=128)
                        ).then_inc(SG[i % 2].h, 16)

                @blk.tensor
                def _(t):
                    for i in range(NKC):
                        if i == 0:
                            t.wait_ge(SMg.h, M0 + 16)
                        t.wait_ge(SG[i % 2].h, GB[i % 2] + 16 * (i // 2 + 1))
                        if i >= 1:
                            t.wait_ge(s_a1.h, A1_0 + 2 * i)
                        for a in range(DT):
                            t.matmul(kp[:, :], wkv[:, a, 0:128], xtmp[i % 2][:, a, :],
                                     start=(a == 0), stop=(a == DT - 1))
                        for a in range(DT):
                            mm = t.matmul(vp[:, :], wkv[:, a, 128:256], xtmp[i % 2][:, a, :],
                                          start=(a == 0), stop=(a == DT - 1))
                        mm.then_inc(s_t1.h, 1)

                @blk.scalar
                def _(sc):
                    for i in range(NKC):
                        sc.wait_ge(s_t1.h, T1_0 + i + 1)
                        sc.activation(KT[:, i * CH:(i + 1) * CH], kp[:, :], AF.Identity,
                                      bias=bkv[:, 0:1]).then_inc(s_a1.h, 1)
                        sc.activation(VTfm[:, i * CH:(i + 1) * CH], vp[:, :], AF.Identity,
                                      bias=bkv[:, 1:2]).then_inc(s_a1.h, 1)
            SMg.n += 16
            SG[0].n += 64
            SG[1].n += 64
            s_t1.n += NKC
            s_a1.n += 2 * NKC

            with ExitStack() as ph2:
                tp = [ph2.enter_context(nc.psum_tensor(f"tp{i}", [128, 128], BF16))
                      for i in range(2)]
                T2_0, V1_0 = s_t2.n, s_v1.n
                with nc.Block("p0b") as blk:
                    @blk.tensor
                    def _(t):
                        for j in range(DT):
                            if j >= 2:
                                t.wait_ge(s_v1.h, V1_0 + j - 1)
                            t.transpose(tp[j % 2][:], VTfm[:, j * 128:(j + 1) * 128],
                                        ident[:]).then_inc(s_t2.h, 1)

                    @blk.vector
                    def _(v):
                        for j in range(DT):
                            v.wait_ge(s_t2.h, T2_0 + j + 1)
                            v.tensor_copy(Vt[:, j, :], tp[j % 2][:]).then_inc(s_v1.h, 1)
                s_t2.n += DT
                s_v1.n += DT

        if MAX_PHASE < 3:
            return finish()
        with ExitStack() as ph:
            xo = ph.enter_context(nc.sbuf_tensor("xo", [128, DT, TOWN], BF16))
            wqs = [ph.enter_context(nc.sbuf_tensor(f"wqs{i}", [128, DT, 128], BF16))
                   for i in range(2)]
            qst = [ph.enter_context(nc.sbuf_tensor(f"qst{i}", [128, TOWN], BF16))
                   for i in range(2)]
            qp = [ph.enter_context(nc.psum_tensor(f"qp{i}", [128, TOWN], F32))
                  for i in range(2)]
            M0 = SMg.n
            GB = [SG[0].n, SG[1].n]
            OB = [SO[0].n, SO[1].n]
            T1_0, A1_0 = s_t1.n, s_a1.n

            with nc.Block("p1") as blk:
                @blk.gpsimd
                def _(g):
                    for j in range(4):
                        g.dma_start(
                            xo[:, 8 * j:8 * (j + 1), :],
                            xTown[8 * j * 128:8 * (j + 1) * 128, :].rearrange(
                                "(a p) u -> p a u", p=128)
                        ).then_inc(SMg.h, 16)
                    for o in range(DT):
                        if o >= 2:
                            g.wait_ge(s_t1.h, T1_0 + o - 1)
                        g.dma_start(
                            wqs[o % 2][:],
                            wqT[:, o * 128:(o + 1) * 128].rearrange("(a p) o -> p a o", p=128)
                        ).then_inc(SG[o % 2].h, 16)

                @blk.tensor
                def _(t):
                    for o in range(DT):
                        if o == 0:
                            t.wait_ge(SMg.h, M0 + 64)
                        t.wait_ge(SG[o % 2].h, GB[o % 2] + 16 * (o // 2 + 1))
                        if o >= 2:
                            t.wait_ge(s_a1.h, A1_0 + o - 1)
                        for a in range(DT):
                            for nb in range(4):
                                mm = t.matmul(qp[o % 2][:, nb * 512:(nb + 1) * 512],
                                              wqs[o % 2][:, a, :],
                                              xo[:, a, nb * 512:(nb + 1) * 512],
                                              start=(a == 0), stop=(a == DT - 1))
                        mm.then_inc(s_t1.h, 1)

                @blk.scalar
                def _(sc):
                    for o in range(DT):
                        sc.wait_ge(s_t1.h, T1_0 + o + 1)
                        if o >= 2:
                            sc.wait_ge(SO[o % 2].h, OB[o % 2] + 64 * (o // 2))
                        sc.activation(qst[o % 2][:], qp[o % 2][:], AF.Identity,
                                      bias=bq[:, o:o + 1]).then_inc(s_a1.h, 1)

                @blk.sync
                def _(sy):
                    for o in range(DT):
                        sy.wait_ge(s_a1.h, A1_0 + o + 1)
                        for c in range(NCH):
                            sy.dma_start(
                                qspc[c:c + 1, o:o + 1, :, :].rearrange(
                                    "c o p q -> p (c o) q"),
                                qst[o % 2][:, c * CH:(c + 1) * CH]
                            ).then_inc(SO[o % 2].h, 16)
                    sy.wait_ge(SO[0].h, OB[0] + 1024)
                    sy.wait_ge(SO[1].h, OB[1] + 1024)
            SMg.n += 64
            SG[0].n += 256
            SG[1].n += 256
            SO[0].n += 1024
            SO[1].n += 1024
            s_t1.n += DT
            s_a1.n += DT

        if MAX_PHASE < 4:
            return finish()
        es_aor = ExitStack()
        aux_stacks.append(es_aor)
        aor = es_aor.enter_context(nc.sbuf_tensor("aor", [128, DT, TOWN], BF16))
        with ExitStack() as ph:
            QR = [ph.enter_context(nc.sbuf_tensor(f"qr{i}", [128, 8, CH], BF16))
                  for i in range(3)]
            mk = [ph.enter_context(nc.sbuf_tensor(f"mk{i}", [128, MW * CH], BF16))
                  for i in range(2)]
            pb = [ph.enter_context(nc.sbuf_tensor(f"pb{i}", [128, 1024], BF16))
                  for i in range(4)]
            rec = [ph.enter_context(nc.sbuf_tensor(f"rec{i}", [128, CH], F32))
                   for i in range(2)]
            sp = [ph.enter_context(nc.psum_tensor(f"sp{i}", [128, 1024], F32))
                  for i in range(2)]
            op = [ph.enter_context(nc.psum_tensor(f"op{i}", [128, CH], F32))
                  for i in range(2)]
            db = [ph.enter_context(nc.psum_tensor(f"db{i}", [128, CH], F32))
                  for i in range(2)]

            QRB = [SQR[0].n, SQR[1].n, SQR[2].n]
            MB = SMg.n
            T1_0, T5_0 = s_t1.n, s_t5.n
            A1_0 = s_a1.n
            V1_0, V2_0 = s_v1.n, s_v2.n

            n_groups = sum(e // 2 for e in EXTS[0]) * H
            assert n_groups == sum(e // 2 for e in EXTS[1]) * H
            NHEAD = NCH * H

            def par_tabs(par):
                GRP = [e // 2 for e in EXTS[par]]
                SCUM = [0]
                for cc in range(NCH):
                    SCUM.append(SCUM[-1] + H * GRP[cc])
                GCUM = []
                tot = 0
                for cc in range(NCH):
                    for h in range(H):
                        tot += GRP[cc]
                        GCUM.append(tot)
                return GRP, SCUM, GCUM

            with nc.Block("p2") as blk:
                @blk.sync
                def _(sy):
                    idx = sy.partition_id() % 2
                    for par in _par_iter(sy, idx):
                        GRP, SCUM, GCUM = par_tabs(par)
                        for l in range(4 * NCH):
                            c, hg = l // 4, l % 4
                            if l >= 3:
                                lc, lhg = (l - 3) // 4, (l - 3) % 4
                                sy.wait_ge(s_t1.h,
                                           T1_0 + SCUM[lc] + (lhg + 1) * 8 * GRP[lc])
                            sy.dma_start(
                                QR[l % 3][:],
                                qspc[c:c + 1, hg * 8:(hg + 1) * 8, :, :].rearrange(
                                    "c h p q -> p (c h) q")
                            ).then_inc(SQR[l % 3].h, 16)

                @blk.gpsimd
                def _(g):
                    idx = g.partition_id() % 2
                    for par in _par_iter(g, idx):
                        GRP, SCUM, GCUM = par_tabs(par)
                        for c in range(NCH):
                            if c >= 2:
                                g.wait_ge(s_t1.h, T1_0 + SCUM[c - 1])
                            g.dma_start(
                                mk[c % 2][:],
                                masksd[c:c + 1, :, :].rearrange("c p w -> p (c w)")
                            ).then_inc(SMg.h, 16)

                @blk.scalar
                def _(sc):
                    idx = sc.partition_id() % 2
                    for par in _par_iter(sc, idx):
                        GRP, SCUM, GCUM = par_tabs(par)
                        gg = 0
                        for c in range(NCH):
                            Gc = GRP[c]
                            for h in range(H):
                                for g in range(Gc):
                                    sc.wait_ge(s_t1.h, T1_0 + gg + 1)
                                    if gg >= 4:
                                        sc.wait_ge(s_t5.h, T5_0 + gg - 3)
                                    sc.activation(pb[gg % 4][:], sp[gg % 2][:], AF.Exp,
                                                  bias=eshift[:, 0:1], scale=SCALE
                                                  ).then_inc(s_a1.h, 1)
                                    gg += 1

                @blk.tensor
                def _(t):
                    idx = t.partition_id() % 2
                    for par in _par_iter(t, idx):
                        GRP, SCUM, GCUM = par_tabs(par)
                        gg = 0
                        hd = 0
                        for c in range(NCH):
                            Gc = GRP[c]
                            for h in range(H):
                                l = 4 * c + h // 8
                                if h % 8 == 0:
                                    t.wait_ge(SQR[l % 3].h,
                                              QRB[l % 3] + 16 * (l // 3 + 1))
                                qa_ap = QR[l % 3][:, h % 8, :]
                                first_pgg = gg

                                def pv_group(j, base_gg, Gc=None, hd=None):
                                    pj = base_gg + j
                                    t.wait_ge(s_a1.h, A1_0 + pj + 1)
                                    if j == 0 and hd >= 2:
                                        t.wait_ge(s_v1.h, V1_0 + hd - 1)
                                    last = (j == Gc - 1)
                                    t.matmul(op[hd % 2][:, :], Vt[:, 2 * j, :],
                                             pb[pj % 4][:, 0:512],
                                             start=(j == 0), stop=False)
                                    t.matmul(op[hd % 2][:, :], Vt[:, 2 * j + 1, :],
                                             pb[pj % 4][:, 512:1024],
                                             start=False, stop=last)
                                    t.matmul(db[hd % 2][:, :], ones_mat[:],
                                             pb[pj % 4][:, 0:512],
                                             start=(j == 0), stop=False)
                                    t.matmul(db[hd % 2][:, :], ones_mat[:],
                                             pb[pj % 4][:, 512:1024],
                                             start=False, stop=last
                                             ).then_inc(s_t5.h, 1)

                                for g in range(Gc):
                                    if gg >= 2:
                                        t.wait_ge(s_a1.h, A1_0 + gg - 1)
                                    masked = (g >= Gc - 2)
                                    if masked and g == Gc - 2 and h == 0:
                                        t.wait_ge(SMg.h, MB + 16 * (c + 1))
                                    if masked:
                                        w0 = 2 * (g - (Gc - 2))
                                        t.matmul(sp[gg % 2][:, 0:512],
                                                 KT[:, (2 * g) * 128:(2 * g + 1) * 128],
                                                 qa_ap, start=True, stop=False)
                                        t.matmul(sp[gg % 2][:, 0:512], ident[:],
                                                 mk[c % 2][:, w0 * 512:(w0 + 1) * 512],
                                                 start=False, stop=True)
                                        t.matmul(sp[gg % 2][:, 512:1024],
                                                 KT[:, (2 * g + 1) * 128:
                                                    (2 * g + 2) * 128],
                                                 qa_ap, start=True, stop=False)
                                        t.matmul(sp[gg % 2][:, 512:1024], ident[:],
                                                 mk[c % 2][:,
                                                    (w0 + 1) * 512:(w0 + 2) * 512],
                                                 start=False, stop=True
                                                 ).then_inc(s_t1.h, 1)
                                    else:
                                        t.matmul(sp[gg % 2][:, 0:512],
                                                 KT[:, (2 * g) * 128:(2 * g + 1) * 128],
                                                 qa_ap, start=True, stop=True)
                                        t.matmul(sp[gg % 2][:, 512:1024],
                                                 KT[:, (2 * g + 1) * 128:
                                                    (2 * g + 2) * 128],
                                                 qa_ap, start=True, stop=True
                                                 ).then_inc(s_t1.h, 1)
                                    gg += 1
                                    if g >= 1:
                                        pv_group(g - 1, first_pgg, Gc=Gc, hd=hd)
                                pv_group(Gc - 1, first_pgg, Gc=Gc, hd=hd)
                                hd += 1

                @blk.vector
                def _(v):
                    idx = v.partition_id() % 2
                    for par in _par_iter(v, idx):
                        GRP, SCUM, GCUM = par_tabs(par)
                        hd = 0
                        for c in range(NCH):
                            for h in range(H):
                                v.wait_ge(s_t5.h, T5_0 + GCUM[hd])
                                v.reciprocal(rec[hd % 2][:], db[hd % 2][:]
                                             ).then_inc(s_v2.h, 1)
                                v.wait_ge(s_v2.h, V2_0 + hd + 1)
                                v.tensor_mul(aor[:, h, c * CH:(c + 1) * CH],
                                             op[hd % 2][:], rec[hd % 2][:]
                                             ).then_inc(s_v1.h, 1)
                                hd += 1

            SMg.n += 4 * 16
            SQR[0].n += 16 * len(range(0, 4 * NCH, 3))
            SQR[1].n += 16 * len(range(1, 4 * NCH, 3))
            SQR[2].n += 16 * len(range(2, 4 * NCH, 3))
            s_t1.n += n_groups
            s_t5.n += n_groups
            s_a1.n += n_groups
            s_v1.n += NHEAD
            s_v2.n += NHEAD


        if MAX_PHASE < 5:
            return finish()
        with ExitStack() as ph:
            wos = [ph.enter_context(nc.sbuf_tensor(f"wos{i}", [128, DT, 128], BF16))
                   for i in range(2)]
            xres = [ph.enter_context(nc.sbuf_tensor(f"xres{i}", [128, TOWN], F32))
                    for i in range(2)]
            s1 = ph.enter_context(nc.sbuf_tensor("s1_0", [128, TOWN], F32))
            p1st = [ph.enter_context(nc.sbuf_tensor(f"p1st{i}", [128, TOWN], F32))
                    for i in range(2)]
            wp = [ph.enter_context(nc.psum_tensor(f"wp{i}", [128, TOWN], F32))
                  for i in range(2)]
            GB = [SG[0].n, SG[1].n]
            SB = [SS[0].n, SS[1].n]
            OB = [SO[0].n, SO[1].n]
            VA = s_v1.n
            T1_0, A1_0, V1_0 = s_t1.n, s_a1.n, s_v1.n

            with nc.Block("p3") as blk:
                @blk.scalar
                def _(sc):
                    sc.dma_start(xres[0][:], xTown[0:128, :]).then_inc(SS[0].h, 16)
                    sc.dma_start(xres[1][:], xTown[128:256, :]).then_inc(SS[1].h, 16)
                    for o in range(DT):
                        sc.wait_ge(s_t1.h, T1_0 + o + 1)
                        if o >= 1:
                            sc.wait_ge(s_v1.h, V1_0 + o)
                        sc.activation(s1[:], wp[o % 2][:], AF.Identity,
                                      bias=bo[:, o:o + 1]).then_inc(s_a1.h, 1)
                        if o + 2 < DT:
                            sc.wait_ge(s_v1.h, V1_0 + o + 1)
                            sc.dma_start(xres[o % 2][:],
                                         xTown[(o + 2) * 128:(o + 3) * 128, :]
                                         ).then_inc(SS[o % 2].h, 16)

                @blk.gpsimd
                def _(g):
                    for o in range(DT):
                        if o >= 2:
                            g.wait_ge(s_t1.h, T1_0 + o - 1)
                        g.dma_start(
                            wos[o % 2][:],
                            woT[:, o * 128:(o + 1) * 128].rearrange("(a p) o -> p a o", p=128)
                        ).then_inc(SG[o % 2].h, 16)

                @blk.tensor
                def _(t):
                    for o in range(DT):
                        if o == 0:
                            t.wait_ge(s_v1.h, VA)
                        t.wait_ge(SG[o % 2].h, GB[o % 2] + 16 * (o // 2 + 1))
                        if o >= 2:
                            t.wait_ge(s_a1.h, A1_0 + o - 1)
                        for a in range(DT):
                            for nb in range(4):
                                mm = t.matmul(wp[o % 2][:, nb * 512:(nb + 1) * 512],
                                              wos[o % 2][:, a, :],
                                              aor[:, a, nb * 512:(nb + 1) * 512],
                                              start=(a == 0), stop=(a == DT - 1))
                        mm.then_inc(s_t1.h, 1)

                @blk.vector
                def _(v):
                    for o in range(DT):
                        v.wait_ge(s_a1.h, A1_0 + o + 1)
                        v.wait_ge(SS[o % 2].h, SB[o % 2] + 16 * (o // 2 + 1))
                        if o >= 2:
                            v.wait_ge(SO[o % 2].h, OB[o % 2] + 16 * (o // 2))
                        v.tensor_add(p1st[o % 2][:], s1[:], xres[o % 2][:]
                                     ).then_inc(s_v1.h, 1)

                @blk.sync
                def _(sy):
                    for o in range(DT):
                        sy.wait_ge(s_v1.h, V1_0 + o + 1)
                        sy.dma_start(p1sp[o * 128:(o + 1) * 128, :], p1st[o % 2][:]
                                     ).then_inc(SO[o % 2].h, 16)
                    sy.wait_ge(SO[0].h, OB[0] + 256)
                    sy.wait_ge(SO[1].h, OB[1] + 256)
            SG[0].n += 256
            SG[1].n += 256
            SS[0].n += 256
            SS[1].n += 256
            SO[0].n += 256
            SO[1].n += 256
            s_t1.n += DT
            s_a1.n += DT
            s_v1.n += DT
        es_aor.close()
        aux_stacks.remove(es_aor)
        es_kv.close()
        aux_stacks.remove(es_kv)

        def ln_stats(src, RSb, MRSb, tag):
            with ExitStack() as ph:
                prb = [ph.enter_context(nc.sbuf_tensor(f"prb{tag}{i}", [128, TOWN], BF16))
                       for i in range(2)]
                sqb = [ph.enter_context(nc.sbuf_tensor(f"sqb{tag}{i}", [128, TOWN], BF16))
                       for i in range(2)]
                mean = ph.enter_context(nc.sbuf_tensor(f"mean{tag}", [128, TOWN], F32))
                tmp = ph.enter_context(nc.sbuf_tensor(f"tmp{tag}", [128, TOWN], F32))
                GB = [SG[0].n, SG[1].n]
                T1_0, A1_0 = s_t1.n, s_a1.n

                with ExitStack() as pp:
                    sum_ps = pp.enter_context(nc.psum_tensor(f"sum{tag}", [128, TOWN], F32))
                    sq_ps = pp.enter_context(nc.psum_tensor(f"sq{tag}", [128, TOWN], F32))
                    with nc.Block(f"lnS{tag}") as blk:
                        @blk.gpsimd
                        def _(g):
                            for a in range(DT):
                                if a >= 2:
                                    g.wait_ge(s_t1.h, T1_0 + a - 1)
                                g.dma_start(prb[a % 2][:], src[a * 128:(a + 1) * 128, :]
                                            ).then_inc(SG[a % 2].h, 16)

                        @blk.scalar
                        def _(sc):
                            for a in range(DT):
                                sc.wait_ge(SG[a % 2].h, GB[a % 2] + 16 * (a // 2 + 1))
                                if a >= 2:
                                    sc.wait_ge(s_t1.h, T1_0 + a - 1)
                                sc.activation(sqb[a % 2][:], prb[a % 2][:], AF.Square
                                              ).then_inc(s_a1.h, 1)

                        @blk.tensor
                        def _(t):
                            for a in range(DT):
                                t.wait_ge(s_a1.h, A1_0 + a + 1)
                                for j in range(4):
                                    t.matmul(sum_ps[:, j * 512:(j + 1) * 512], ones_mat[:],
                                             prb[a % 2][:, j * 512:(j + 1) * 512],
                                             start=(a == 0), stop=(a == DT - 1))
                                for j in range(4):
                                    mm = t.matmul(sq_ps[:, j * 512:(j + 1) * 512], ones_mat[:],
                                                  sqb[a % 2][:, j * 512:(j + 1) * 512],
                                                  start=(a == 0), stop=(a == DT - 1))
                                mm.then_inc(s_t1.h, 1)
                    SG[0].n += 256
                    SG[1].n += 256
                    s_a1.n += DT
                    s_t1.n += DT

                    with nc.Block(f"lnF1{tag}") as blk:
                        @blk.vector
                        def _(v):
                            def fence(instr):
                                instr.then_inc(s_v3.h, 1)
                                s_v3.n += 1
                                v.wait_ge(s_v3.h, s_v3.n)
                            fence(v.tensor_scalar_mul(mean[:], sum_ps[:], 1.0 / D))
                            fence(v.tensor_scalar_mul(tmp[:], sq_ps[:], 1.0 / D))
                            fence(v.tensor_mul(RSb[:], mean[:], mean[:]))
                            fence(v.tensor_sub(tmp[:], tmp[:], RSb[:]))
                            fence(v.tensor_scalar_add(tmp[:], tmp[:], EPS))
                            v.reciprocal(RSb[:], tmp[:])

                with nc.Block(f"lnF2{tag}") as blk:
                    @blk.scalar
                    def _(sc):
                        sc.activation(tmp[:], RSb[:], AF.Sqrt)

                with nc.Block(f"lnF3{tag}") as blk:
                    @blk.vector
                    def _(v):
                        v.tensor_copy(RSb[:], tmp[:]).then_inc(s_v3.h, 1)
                        s_v3.n += 1
                        v.wait_ge(s_v3.h, s_v3.n)
                        v.tensor_mul(MRSb[:], mean[:], RSb[:])

        if MAX_PHASE < 6:
            return finish()
        es_rs1 = ExitStack()
        aux_stacks.append(es_rs1)
        RS1b = es_rs1.enter_context(nc.sbuf_tensor("RS1b", [128, TOWN], F32))
        MRS1b = es_rs1.enter_context(nc.sbuf_tensor("MRS1b", [128, TOWN], F32))
        ln_stats(p1sp, RS1b, MRS1b, "1")

        if MAX_PHASE < 7:
            return finish()
        es_y1 = ExitStack()
        aux_stacks.append(es_y1)
        y1 = es_y1.enter_context(nc.sbuf_tensor("y1", [128, DT, TOWN], BF16))
        with ExitStack() as ph:
            pr32 = [ph.enter_context(nc.sbuf_tensor(f"pr32a{i}", [128, TOWN], F32))
                    for i in range(2)]
            t1b = ph.enter_context(nc.sbuf_tensor("t1a", [128, TOWN], F32))
            SB = [SS[0].n, SS[1].n]
            V1_0 = s_v1.n
            with nc.Block("ln1a") as blk:
                @blk.scalar
                def _(sc):
                    for a in range(DT):
                        if a >= 2:
                            sc.wait_ge(s_v1.h, V1_0 + a - 1)
                        sc.dma_start(pr32[a % 2][:], p1sp[a * 128:(a + 1) * 128, :]
                                     ).then_inc(SS[a % 2].h, 16)

                @blk.vector
                def _(v):
                    for a in range(DT):
                        v.wait_ge(SS[a % 2].h, SB[a % 2] + 16 * (a // 2 + 1))
                        if a >= 1:
                            v.wait_ge(s_v1.h, V1_0 + a)
                        v.tensor_mul(t1b[:], pr32[a % 2][:], RS1b[:]).then_inc(s_v3.h, 1)
                        s_v3.n += 1
                        v.wait_ge(s_v3.h, s_v3.n)
                        v.tensor_sub(y1[:, a, :], t1b[:], MRS1b[:]).then_inc(s_v1.h, 1)
            SS[0].n += 256
            SS[1].n += 256
            s_v1.n += DT

        if MAX_PHASE < 8:
            return finish()
        with ExitStack() as ph:
            w1s = [ph.enter_context(nc.sbuf_tensor(f"w1s{i}", [128, DT, 128], BF16))
                   for i in range(4)]
            sl = ph.enter_context(nc.sbuf_tensor("sl", [128, 1024], F32))
            h2s = ph.enter_context(nc.sbuf_tensor("h2s", [128, 1024], F32))
            x4st = [ph.enter_context(nc.sbuf_tensor(f"x4st{i}", [128, TOWN], BF16))
                    for i in range(2)]
            hp = [ph.enter_context(nc.psum_tensor(f"hp{i}", [128, 1024], F32))
                  for i in range(4)]
            GB = [SG[0].n, SG[1].n]
            OB = [SO[0].n, SO[1].n]
            T1_0, A1_0, V1_0 = s_t1.n, s_a1.n, s_v1.n

            with nc.Block("p4") as blk:
                @blk.gpsimd
                def _(g):
                    for i in range(DT):
                        if i >= 2:
                            g.wait_ge(s_t1.h, T1_0 + 2 * (i - 1))
                        for k, col in ((0, i), (1, 32 + i)):
                            g.dma_start(
                                w1s[2 * (i % 2) + k][:],
                                w12T[:, col * 128:(col + 1) * 128].rearrange(
                                    "(a p) o -> p a o", p=128)
                            ).then_inc(SG[i % 2].h, 16)

                @blk.tensor
                def _(t):
                    for i in range(DT):
                        t.wait_ge(SG[i % 2].h, GB[i % 2] + 32 * (i // 2 + 1))
                        for hf in range(2):
                            hi = 2 * i + hf
                            if hi >= 2:
                                t.wait_ge(s_a1.h, A1_0 + 2 * (hi - 1))
                            for a in range(DT):
                                for nb in range(2):
                                    t.matmul(hp[hf][:, nb * 512:(nb + 1) * 512],
                                             w1s[2 * (i % 2)][:, a, :],
                                             y1[:, a, hf * 1024 + nb * 512:
                                                hf * 1024 + (nb + 1) * 512],
                                             start=(a == 0), stop=(a == DT - 1))
                            for a in range(DT):
                                for nb in range(2):
                                    mm = t.matmul(hp[2 + hf][:, nb * 512:(nb + 1) * 512],
                                                  w1s[2 * (i % 2) + 1][:, a, :],
                                                  y1[:, a, hf * 1024 + nb * 512:
                                                     hf * 1024 + (nb + 1) * 512],
                                                  start=(a == 0), stop=(a == DT - 1))
                            mm.then_inc(s_t1.h, 1)

                @blk.scalar
                def _(sc):
                    for i in range(DT):
                        for hf in range(2):
                            hi = 2 * i + hf
                            sc.wait_ge(s_t1.h, T1_0 + hi + 1)
                            if hi >= 1:
                                sc.wait_ge(s_v1.h, V1_0 + hi)
                            sc.activation(sl[:], hp[hf][:], AF.Silu,
                                          bias=b12[:, i:i + 1]).then_inc(s_a1.h, 1)
                            sc.activation(h2s[:], hp[2 + hf][:], AF.Identity,
                                          bias=b12[:, 32 + i:33 + i]).then_inc(s_a1.h, 1)

                @blk.vector
                def _(v):
                    for i in range(DT):
                        for hf in range(2):
                            hi = 2 * i + hf
                            v.wait_ge(s_a1.h, A1_0 + 2 * (hi + 1))
                            if hf == 0 and i >= 2:
                                v.wait_ge(SO[i % 2].h, OB[i % 2] + 16 * (i // 2))
                            v.tensor_mul(x4st[i % 2][:, hf * 1024:(hf + 1) * 1024],
                                         sl[:], h2s[:]).then_inc(s_v1.h, 1)

                @blk.sync
                def _(sy):
                    for i in range(DT):
                        sy.wait_ge(s_v1.h, V1_0 + 2 * (i + 1))
                        sy.dma_start(x4sp[i * 128:(i + 1) * 128, :], x4st[i % 2][:]
                                     ).then_inc(SO[i % 2].h, 16)
                    sy.wait_ge(SO[0].h, OB[0] + 256)
                    sy.wait_ge(SO[1].h, OB[1] + 256)
            SG[0].n += 512
            SG[1].n += 512
            SO[0].n += 256
            SO[1].n += 256
            s_t1.n += 2 * DT
            s_a1.n += 4 * DT
            s_v1.n += 2 * DT
        es_y1.close()
        aux_stacks.remove(es_y1)

        if MAX_PHASE < 9:
            return finish()
        with ExitStack() as ph:
            x4r = ph.enter_context(nc.sbuf_tensor("x4r", [128, DT, TOWN], BF16))
            w3s = [ph.enter_context(nc.sbuf_tensor(f"w3s{i}", [128, DT, 128], BF16))
                   for i in range(2)]
            pr32 = [ph.enter_context(nc.sbuf_tensor(f"pr32b{i}", [128, TOWN], F32))
                    for i in range(2)]
            s1 = ph.enter_context(nc.sbuf_tensor("s1b", [128, TOWN], F32))
            p2st = [ph.enter_context(nc.sbuf_tensor(f"p2st{i}", [128, TOWN], F32))
                    for i in range(2)]
            wp = [ph.enter_context(nc.psum_tensor(f"wp5{i}", [128, TOWN], F32))
                  for i in range(2)]
            M0 = SMs.n
            GB = [SG[0].n, SG[1].n]
            SB = [SS[0].n, SS[1].n]
            OB = [SO[0].n, SO[1].n]
            T1_0, A1_0, V1_0 = s_t1.n, s_a1.n, s_v1.n

            with nc.Block("p5") as blk:
                @blk.scalar
                def _(sc):
                    for j in range(4):
                        sc.dma_start(
                            x4r[:, 8 * j:8 * (j + 1), :],
                            x4sp[8 * j * 128:8 * (j + 1) * 128, :].rearrange(
                                "(a p) u -> p a u", p=128)
                        ).then_inc(SMs.h, 16)
                    sc.dma_start(pr32[0][:], p1sp[0:128, :]).then_inc(SS[0].h, 16)
                    sc.dma_start(pr32[1][:], p1sp[128:256, :]).then_inc(SS[1].h, 16)
                    for o in range(DT):
                        sc.wait_ge(s_t1.h, T1_0 + o + 1)
                        if o >= 1:
                            sc.wait_ge(s_v1.h, V1_0 + o)
                        sc.activation(s1[:], wp[o % 2][:], AF.Identity,
                                      bias=b3[:, o:o + 1]).then_inc(s_a1.h, 1)
                        if o + 2 < DT:
                            sc.wait_ge(s_v1.h, V1_0 + o + 1)
                            sc.dma_start(pr32[o % 2][:],
                                         p1sp[(o + 2) * 128:(o + 3) * 128, :]
                                         ).then_inc(SS[o % 2].h, 16)

                @blk.gpsimd
                def _(g):
                    for o in range(DT):
                        if o >= 2:
                            g.wait_ge(s_t1.h, T1_0 + o - 1)
                        g.dma_start(
                            w3s[o % 2][:],
                            w3T[:, o * 128:(o + 1) * 128].rearrange("(a p) o -> p a o", p=128)
                        ).then_inc(SG[o % 2].h, 16)

                @blk.tensor
                def _(t):
                    for o in range(DT):
                        if o == 0:
                            t.wait_ge(SMs.h, M0 + 64)
                        t.wait_ge(SG[o % 2].h, GB[o % 2] + 16 * (o // 2 + 1))
                        if o >= 2:
                            t.wait_ge(s_a1.h, A1_0 + o - 1)
                        for a in range(DT):
                            for nb in range(4):
                                mm = t.matmul(wp[o % 2][:, nb * 512:(nb + 1) * 512],
                                              w3s[o % 2][:, a, :],
                                              x4r[:, a, nb * 512:(nb + 1) * 512],
                                              start=(a == 0), stop=(a == DT - 1))
                        mm.then_inc(s_t1.h, 1)

                @blk.vector
                def _(v):
                    for o in range(DT):
                        v.wait_ge(s_a1.h, A1_0 + o + 1)
                        v.wait_ge(SS[o % 2].h, SB[o % 2] + 16 * (o // 2 + 1))
                        if o >= 2:
                            v.wait_ge(SO[o % 2].h, OB[o % 2] + 16 * (o // 2))
                            v.wait_ge(s_v1.h, V1_0 + o - 1)
                        v.tensor_mul(p2st[o % 2][:], pr32[o % 2][:], RS1b[:]
                                     ).then_inc(s_v3.h, 1)
                        s_v3.n += 1
                        v.wait_ge(s_v3.h, s_v3.n)
                        v.tensor_sub(p2st[o % 2][:], p2st[o % 2][:], MRS1b[:]
                                     ).then_inc(s_v3.h, 1)
                        s_v3.n += 1
                        v.wait_ge(s_v3.h, s_v3.n)
                        v.tensor_add(p2st[o % 2][:], p2st[o % 2][:], s1[:]
                                     ).then_inc(s_v1.h, 1)

                @blk.sync
                def _(sy):
                    for o in range(DT):
                        sy.wait_ge(s_v1.h, V1_0 + o + 1)
                        sy.dma_start(p2sp[o * 128:(o + 1) * 128, :], p2st[o % 2][:]
                                     ).then_inc(SO[o % 2].h, 16)
                    sy.wait_ge(SO[0].h, OB[0] + 256)
                    sy.wait_ge(SO[1].h, OB[1] + 256)
            SMs.n += 64
            SG[0].n += 256
            SG[1].n += 256
            SS[0].n += 256
            SS[1].n += 256
            SO[0].n += 256
            SO[1].n += 256
            s_t1.n += DT
            s_a1.n += DT
            s_v1.n += DT
        es_rs1.close()
        aux_stacks.remove(es_rs1)

        if MAX_PHASE < 10:
            return finish()
        es_rs2 = ExitStack()
        aux_stacks.append(es_rs2)
        RS2b = es_rs2.enter_context(nc.sbuf_tensor("RS2b", [128, TOWN], F32))
        MRS2b = es_rs2.enter_context(nc.sbuf_tensor("MRS2b", [128, TOWN], F32))
        ln_stats(p2sp, RS2b, MRS2b, "2")

        if MAX_PHASE < 11:
            return finish()
        with ExitStack() as ph:
            pr32 = [ph.enter_context(nc.sbuf_tensor(f"pr32c{i}", [128, TOWN], F32))
                    for i in range(2)]
            outst = [ph.enter_context(nc.sbuf_tensor(f"outst{i}", [128, TOWN], F32))
                     for i in range(2)]
            SB = [SS[0].n, SS[1].n]
            OB = [SO[0].n, SO[1].n]
            V1_0 = s_v1.n
            with nc.Block("ln2a") as blk:
                @blk.scalar
                def _(sc):
                    for a in range(DT):
                        if a >= 2:
                            sc.wait_ge(s_v1.h, V1_0 + a - 1)
                        sc.dma_start(pr32[a % 2][:], p2sp[a * 128:(a + 1) * 128, :]
                                     ).then_inc(SS[a % 2].h, 16)

                @blk.vector
                def _(v):
                    for a in range(DT):
                        v.wait_ge(SS[a % 2].h, SB[a % 2] + 16 * (a // 2 + 1))
                        if a >= 2:
                            v.wait_ge(SO[a % 2].h, OB[a % 2] + 16 * (a // 2))
                            v.wait_ge(s_v1.h, V1_0 + a - 1)
                        v.tensor_mul(outst[a % 2][:], pr32[a % 2][:], RS2b[:]
                                     ).then_inc(s_v3.h, 1)
                        s_v3.n += 1
                        v.wait_ge(s_v3.h, s_v3.n)
                        v.tensor_sub(outst[a % 2][:], outst[a % 2][:], MRS2b[:]
                                     ).then_inc(s_v1.h, 1)

                @blk.sync
                def _(sy):
                    for a in range(DT):
                        sy.wait_ge(s_v1.h, V1_0 + a + 1)
                        sy.dma_start(outT[a * 128:(a + 1) * 128, :], outst[a % 2][:]
                                     ).then_inc(SO[a % 2].h, 16)
                    sy.wait_ge(SO[0].h, OB[0] + 256)
                    sy.wait_ge(SO[1].h, OB[1] + 256)
            SS[0].n += 256
            SS[1].n += 256
            SO[0].n += 256
            SO[1].n += 256
            s_v1.n += DT
        es_rs2.close()
        aux_stacks.remove(es_rs2)

    return nc


def _make_masks(parity):
    masks = np.zeros((NCH, 128, MW * CH), dtype=np.float32)
    q = np.arange(CH)[None, :]
    k = np.arange(128)[:, None]
    for c in range(NCH):
        q0 = Q0S[parity][c]
        for w in range(MW):
            kt = EXTS[parity][c] - MW + w
            keep = (q0 + q) >= (kt * 128 + k)
            masks[c, :, w * CH:(w + 1) * CH] = np.where(keep, 0.0, -1.0e5)
    return masks


def kernel(L0_attn_x, L0_attn_wq_w, L0_attn_wq_b, L0_attn_wk_w, L0_attn_wk_b,
           L0_attn_wv_w, L0_attn_wv_b, L0_attn_wo_w, L0_attn_wo_b,
           L0_ffn_w12, L0_ffn_b12, L0_ffn_w3, L0_ffn_b3):
    x = np.asarray(L0_attn_x, dtype=np.float32)

    c = np.ascontiguousarray
    wqT = c(np.asarray(L0_attn_wq_w, np.float32).T)
    wkvT = c(np.concatenate([np.asarray(L0_attn_wk_w, np.float32).T,
                             np.asarray(L0_attn_wv_w, np.float32).T], axis=1))
    woT = c(np.asarray(L0_attn_wo_w, np.float32).T)
    w12T = c(np.asarray(L0_ffn_w12, np.float32).T)
    w3T = c(np.asarray(L0_ffn_w3, np.float32).T)
    bq = c(np.asarray(L0_attn_wq_b, np.float32).reshape(32, 128).T)
    bo = c(np.asarray(L0_attn_wo_b, np.float32).reshape(32, 128).T)
    b12 = c(np.asarray(L0_ffn_b12, np.float32).reshape(64, 128).T)
    b3 = c(np.asarray(L0_ffn_b3, np.float32).reshape(32, 128).T)
    bkv = c(np.stack([np.asarray(L0_attn_wk_b, np.float32),
                      np.asarray(L0_attn_wv_b, np.float32)], axis=1))
    ident = np.eye(128, dtype=np.float32)
    masks = [_make_masks(0), _make_masks(1)]

    nc = build_program()

    in_maps = []
    for core in range(8):
        b = core // 2
        parity = core % 2
        xTb = c(x[b].T)
        if parity == 0:
            xTown = c(np.concatenate([xTb[:, 0:1024], xTb[:, 3072:4096]], axis=1))
        else:
            xTown = c(xTb[:, 1024:3072])
        in_maps.append({
            "xT": xTb, "xTown": xTown, "wqT": wqT, "wkvT": wkvT, "woT": woT,
            "w12T": w12T, "w3T": w3T, "bqw": bq, "bow": bo, "b12w": b12, "b3w": b3,
            "bkvw": bkv, "masks": masks[parity], "identw": ident,
        })

    res = run_bass_kernel_spmd(nc, in_maps, core_ids=list(range(8)))
    kernel._last_results = res

    out = np.empty((BATCH, T, D), dtype=np.float32)
    for core in range(8):
        b = core // 2
        parity = core % 2
        oT = res.results[core]["outT"]
        if parity == 0:
            out[b, 0:1024, :] = oT[:, 0:1024].T
            out[b, 3072:4096, :] = oT[:, 1024:2048].T
        else:
            out[b, 1024:3072, :] = oT.T
    return out
